# version 07 (final)
# baseline (speedup 1.0000x reference)
import numpy as np
import ml_dtypes

import concourse.bass as bass
import concourse.mybir as mybir
from concourse.bass_utils import run_bass_kernel_spmd

F32 = mybir.dt.float32
BF16 = mybir.dt.bfloat16
I32 = mybir.dt.int32
AF = mybir.ActivationFunctionType
ALU = mybir.AluOpType
AX = mybir.AxisListType
f32 = np.float32

D = 4096
S = 4096
HD = 128
LAT = 1024
H_FFN = 11008
QW = 2752
S_A = 2560
SCALE = 1.0 / float(np.sqrt(HD))
SHIFT = 16.0
THETA = 10000.0
MASK_NEG = -30000.0
EPS = 1e-5

DBG_FFN = False
T_A, T_B = S_A, S - S_A
TCD = {0: T_A // 2, 1: T_B // 2}
OUT_ROWS = TCD[0]
MAGICS = [0x5EED0001 + 7 * i for i in range(16)]



def _rope_tables():
    inv = 1.0 / THETA ** (np.arange(0, HD, 2, dtype=np.float64) / HD)
    t = np.arange(S, dtype=np.float64)
    fr = np.outer(t, inv)
    emb = np.concatenate([fr, fr], axis=1)
    cos = np.cos(emb).astype(np.float32)
    sin = np.sin(emb).astype(np.float32)
    sin_s = sin.copy()
    sin_s[:, :64] *= -1.0
    return np.ascontiguousarray(cos.T), np.ascontiguousarray(sin_s.T)


def _masks():
    out = np.zeros((128, 4 * 512), np.float32)
    for i, o in enumerate((0, 128, 256, 384)):
        k = o + np.arange(128)[:, None]
        q = np.arange(512)[None, :]
        out[:, i * 512:(i + 1) * 512] = np.where(k > q, MASK_NEG, 0.0)
    return out


def _fc_major(w):
    K, F = w.shape
    return np.ascontiguousarray(w.reshape(K, F // 128, 128).transpose(1, 0, 2))


def _quad_major(wT):
    K = wT.shape[0]
    out = np.zeros((22, 4, K, 128), np.float32)
    for c in range(22):
        w_ = 128 if c < 21 else 64
        for m in range(4):
            c0 = 128 * c + QW * m
            out[c, m, :, :w_] = wT[:, c0:c0 + w_]
    return np.ascontiguousarray(out)


def host_prepare(inputs):
    x = np.asarray(inputs["L0_attn_hidden_states"], np.float32)
    w_q = np.asarray(inputs["L0_attn_w_q"], np.float32)
    w_k = np.asarray(inputs["L0_attn_w_k"], np.float32)
    w_v = np.asarray(inputs["L0_attn_w_v"], np.float32)
    w_o = np.asarray(inputs["L0_attn_w_o"], np.float32)
    w_lq = np.asarray(inputs["L0_attn_w_lq"], np.float32)
    w_lk = np.asarray(inputs["L0_attn_w_lk"], np.float32)
    w_lv = np.asarray(inputs["L0_attn_w_lv"], np.float32)
    w_lo = np.asarray(inputs["L0_attn_w_lo"], np.float32)
    w1q = np.asarray(inputs["L0_ffn_w1q"], np.float32)
    a1 = np.asarray(inputs["L0_ffn_a1"], np.float32)
    b1 = np.asarray(inputs["L0_ffn_b1"], np.float32)
    w3q = np.asarray(inputs["L0_ffn_w3q"], np.float32)
    a3 = np.asarray(inputs["L0_ffn_a3"], np.float32)
    b3 = np.asarray(inputs["L0_ffn_b3"], np.float32)
    w2q = np.asarray(inputs["L0_ffn_w2q"], np.float32)
    a2 = np.asarray(inputs["L0_ffn_a2"], np.float32)
    b2 = np.asarray(inputs["L0_ffn_b2"], np.float32)

    cosT, sinT = _rope_tables()
    masks = _masks()
    w_comb = np.concatenate([w_o, w_lo], axis=0)
    w_comb_dm = np.ascontiguousarray(
        w_comb.reshape(8192, D // 128, 128).transpose(1, 0, 2))
    w2qT_pad = np.zeros((2816, D), np.float32)
    w2qT_pad[:QW] = w2q.T
    w2qT_dm = np.ascontiguousarray(
        w2qT_pad.reshape(2816, D // 128, 128).transpose(1, 0, 2))
    w1q_q = _quad_major(w1q.T)
    w3q_q = _quad_major(w3q.T)
    b1_q = np.ascontiguousarray(
        _quad_major(b1.T).transpose(2, 0, 1, 3))
    b3_q = np.ascontiguousarray(
        _quad_major(b3.T).transpose(2, 0, 1, 3))
    a2T = np.ascontiguousarray(a2.T)
    a2_qm = np.zeros((22, 4, 128, 2), np.float32)
    for c in range(22):
        w_ = 128 if c < 21 else 64
        for m in range(4):
            c0 = 128 * c + QW * m
            a2_qm[c, m, :w_, :] = a2T[c0:c0 + w_, :]
    a2_qm = np.ascontiguousarray(a2_qm.transpose(2, 0, 1, 3))
    a1T = np.ascontiguousarray(a1.T)
    a3T = np.ascontiguousarray(a3.T)
    b2T = np.ascontiguousarray(b2.T)

    in_maps = []
    for c in range(8):
        bb = c // 4
        p = c % 2
        qh0 = 16 * p
        kv0 = 4 * p
        role = (c // 2) % 2
        qlo_c = 0 if role == 0 else S_A
        myT = TCD[role]
        t0c = qlo_c + p * myT
        xTcd = np.zeros((D, TCD[0]), np.float32)
        xTcd[:, 0:myT] = x[bb].T[:, t0c:t0c + myT]
        m = {
            "xT": np.ascontiguousarray(x[bb].T),
            "xTcd": xTcd,
            "wq_s": _fc_major(w_q[:, qh0 * HD:(qh0 + 16) * HD]),
            "wk_s": _fc_major(w_k[:, kv0 * HD:(kv0 + 4) * HD]),
            "wv_s": np.ascontiguousarray(w_v[:, kv0 * HD:(kv0 + 4) * HD]),
            "wlq_s": _fc_major(w_lq[:, 512 * p:512 * (p + 1)]),
            "wlk_s": _fc_major(w_lk[:, qh0 * HD:(qh0 + 16) * HD]),
            "wlv_s": np.ascontiguousarray(w_lv[:, qh0 * HD:(qh0 + 16) * HD]),
            "wcomb_dm": w_comb_dm,
            "w2q_dm": w2qT_dm,
            "w1q_q": w1q_q, "w3q_q": w3q_q,
            "b1_q": b1_q, "b3_q": b3_q, "a2_qm": a2_qm,
            "a1T": a1T, "a3T": a3T, "b2T": b2T,
            "cosT": cosT, "sinT": sinT, "masks": masks,
            "pslot": np.array([[p]], np.int32),
        }
        in_maps.append(m)
    return in_maps



class Ctx:
    def __init__(self, nc):
        self.nc = nc
        self.sem = {}
        self.cnt = {}
        self.wcache = {}

    def new_sem(self, name):
        h = self.nc.semaphore(name).__enter__()
        self.sem[name] = h
        self.cnt[name] = 0
        return h

    def inc(self, name, inst, k):
        self.cnt[name] += k
        inst.then_inc(self.sem[name], k)
        return self.cnt[name]

    def val(self, name):
        return self.cnt[name]

    def wait(self, eng, name, v=None):
        self.we(eng, name, self.cnt[name] if v is None else v)

    def we(self, eng, name, v):
        key = (eng.engine, name)
        if v <= 0 or self.wcache.get(key, -1) >= v:
            return
        eng.wait_ge(self.sem[name], v)
        self.wcache[key] = v

    def wreset(self):
        self.wcache = {}

    def join(self):
        nc = self.nc
        for eng in (nc.tensor, nc.vector, nc.scalar, nc.sync, nc.gpsimd):
            for sname in ("mm", "ev", "ac", "scr", "kv", "ow", "ol",
                          "wl", "wl2", "xl", "lq", "cs"):
                self.we(eng, sname, self.cnt.get(sname, 0))


def roles_params(role):
    if role == 0:
        kend, qlo, qhi = S_A, 0, S_A
        passes = [(0, 1024), (1024, 2048), (2048, 2560)]
    else:
        kend, qlo, qhi = S, S_A, S
        passes = [(0, 1024), (1024, 2048), (2048, 3072), (3072, 4096)]
    return kend, qlo, qhi, passes


def emit_pswitch_barrier(ctx, tens, barrier_id, wait_sem, copies_fn=None):
    nc = ctx.nc
    g = nc.gpsimd
    magic_sb = tens["magic_sb"]
    flag_sb = tens["flag_sb"]
    flags = tens["flags_sh"]
    cnt = tens["spin_cnt"]
    nd = tens["spin_nd"]
    base = dict(ctx.cnt)
    saved = None

    def body(p):
        ctx.cnt = dict(base)
        ctx.wait(g, wait_sem)
        if copies_fn is not None:
            for dst, src in copies_fn(p):
                dma = g.dma_start(out=dst, in_=src)
                ctx.inc("lqw", dma, 16)
            ctx.wait(g, "lqw")
        g.memset(magic_sb[:, :], MAGICS[barrier_id])
        dma = g.dma_start(out=flags.ap()[barrier_id:barrier_id + 1, p:p + 1],
                          in_=magic_sb[:, :])
        ctx.inc("fl", dma, 16)
        ctx.we(g, "fl", ctx.val("fl"))
        othf = flags.ap()[barrier_id:barrier_id + 1, 1 - p:2 - p]

        def cond():
            g.reg_add(cnt, cnt, 16)
            d = g.dma_start(out=flag_sb[:, :], in_=othf)
            d.then_inc(ctx.sem["sp"], 16)
            g.wait_ge(ctx.sem["sp"], cnt)
            v = g.value_load(flag_sb[0:1, 0:1])
            g.reg_alu(nd, v, MAGICS[barrier_id], ALU.not_equal)
            return nd

        with g.While(cond):
            g.nop()

    wc = dict(ctx.wcache)
    with g.If_eq(tens["pidx_pool_reg"], 0):
        body(0)
        saved = dict(ctx.cnt)
        wc0 = dict(ctx.wcache)
    with g.Else():
        ctx.wcache = wc
        body(1)
    assert saved == ctx.cnt, "asymmetric p-switch bodies"
    ctx.wcache = {k: min(v, wc0.get(k, 0)) for k, v in ctx.wcache.items()
                  if k in wc0}



def emit_phase_a(ctx, tens, role):
    nc = ctx.nc
    kend, qlo, qhi, passes = roles_params(role)

    R = f"r{role}"
    with (
        nc.sbuf_tensor(f"xbf{R}", [128, 32, 1024], BF16) as xbf,
        nc.sbuf_tensor(f"wslab{R}", [128, 2, 32, 128], BF16) as wslab,
        nc.sbuf_tensor(f"wslab_tm{R}", [128, 32, 512], BF16) as wslab_tm,
        nc.sbuf_tensor(f"lqbf{R}", [128, 8, 1024], BF16) as lqbf,
        nc.sbuf_tensor(f"cos_sb{R}", [128, 1024], F32) as cos_sb,
        nc.sbuf_tensor(f"sin_sb{R}", [128, 1024], F32) as sin_sb,
        nc.sbuf_tensor(f"rtmp1{R}", [128, 512], F32) as rtmp1,
        nc.sbuf_tensor(f"rtmp2{R}", [128, 512], F32) as rtmp2,
        nc.sbuf_tensor(f"stage{R}", [128, 4, 512], BF16) as stage,
        nc.psum_tensor(f"psA0{R}", [128, 512], F32) as psA0,
        nc.psum_tensor(f"psA1{R}", [128, 512], F32) as psA1,
        nc.psum_tensor(f"psA2{R}", [128, 512], F32) as psA2,
        nc.psum_tensor(f"psA3{R}", [128, 512], F32) as psA3,
    ):
        psb = [psA0, psA1, psA2, psA3]
        ps_i = [0]
        st_i = [0]
        ps_free = [0, 0, 0, 0]
        ps_eng = [None, None, None, None]
        stage_free = [("scr", 0)] * 4

        def evac_and_store(ps, tw, rope, coff, dst_slice, dq=None, dsem="scr"):
            sb = st_i[0] % 4
            st_i[0] += 1
            stg = stage[:, sb]
            if rope:
                eng = nc.vector
                ctx.wait(eng, "mm")
                eng.wait_ge(ctx.sem[stage_free[sb][0]], stage_free[sb][1])
                ctx.wait(eng, "cs")
                eng.tensor_mul(rtmp1[:, 0:tw], ps[:, 0:tw],
                               cos_sb[:, coff:coff + tw])
                eng.tensor_mul(rtmp2[0:64, 0:tw], ps[64:128, 0:tw],
                               sin_sb[0:64, coff:coff + tw])
                eng.tensor_mul(rtmp2[64:128, 0:tw], ps[0:64, 0:tw],
                               sin_sb[64:128, coff:coff + tw])
                inst = eng.tensor_add(stg[:, 0:tw], rtmp1[:, 0:tw],
                                      rtmp2[:, 0:tw])
                evv = ctx.inc("ev", inst, 1)
                wsem = "ev"
            else:
                ctx.wait(nc.scalar, "mm")
                nc.scalar.wait_ge(ctx.sem[stage_free[sb][0]], stage_free[sb][1])
                inst = nc.scalar.copy(stg[:, 0:tw], ps[:, 0:tw])
                evv = ctx.inc("ac", inst, 1)
                wsem = "ac"
            q = dq if dq is not None else nc.sync
            q.wait_ge(ctx.sem[wsem], evv)
            dma = q.dma_start(out=dst_slice, in_=stg[:, 0:tw])
            ctx.inc(dsem, dma, 16)
            stage_free[sb] = (dsem, ctx.val(dsem))
            return evv, wsem

        def next_bank():
            pi = ps_i[0] % 4
            ps_i[0] += 1
            ps = psb[pi]
            if ps_eng[pi] is not None:
                nc.tensor.wait_ge(ctx.sem[ps_eng[pi]], ps_free[pi])
            return pi, ps

        def proj_fm(w_blocks, nf, kc_n, x_src, rope, dst_fn, Th, coff=0,
                    dq=None, dsem="scr"):
            ntg = (Th + 511) // 512
            for fc in range(nf):
                wb = fc % 2
                nc.gpsimd.wait_ge(ctx.sem["mm"],
                                  ctx.cnt.get(f"slabfree{wb}", 0))
                dma = nc.gpsimd.dma_start(
                    out=wslab[:, wb, 0:kc_n, :],
                    in_=w_blocks[fc].rearrange("(kc p) f -> p kc f", p=128))
                wv = ctx.inc("wl", dma, 16)
                for tg in range(ntg):
                    tw = min(512, Th - 512 * tg)
                    pi, ps = next_bank()
                    ctx.we(nc.tensor, "wl", wv)
                    ctx.wait(nc.tensor, "xl")
                    for kc in range(kc_n):
                        inst = nc.tensor.matmul(
                            ps[:, 0:tw], wslab[:, wb, kc, :],
                            x_src[:, kc, 512 * tg:512 * tg + tw],
                            start=(kc == 0), stop=(kc == kc_n - 1))
                    mmv = ctx.inc("mm", inst, 1)
                    ctx.cnt[f"slabfree{wb}"] = mmv
                    evv, wsem = evac_and_store(
                        ps, tw, rope, coff + 512 * tg,
                        dst_fn(fc, 512 * tg, tw), dq=dq, dsem=dsem)
                    ps_free[pi], ps_eng[pi] = evv, wsem

        def proj_tm(w_ap, fcols, dst_fn, Th):
            ctx.we(nc.gpsimd, "mm", ctx.cnt.get("slabtmfree", 0))
            dma = nc.gpsimd.dma_start(
                out=wslab_tm[:, :, 0:fcols],
                in_=w_ap.rearrange("(kc p) f -> p kc f", p=128))
            wv = ctx.inc("wl", dma, 16)
            for tt in range(Th // 128):
                pi, ps = next_bank()
                ctx.we(nc.tensor, "wl", wv)
                ctx.wait(nc.tensor, "xl")
                for kc in range(32):
                    inst = nc.tensor.matmul(
                        ps[:, 0:fcols], xbf[:, kc, 128 * tt:128 * (tt + 1)],
                        wslab_tm[:, kc, 0:fcols],
                        start=(kc == 0), stop=(kc == 31))
                mmv = ctx.inc("mm", inst, 1)
                ctx.cnt["slabtmfree"] = mmv
                evv, wsem = evac_and_store(ps, fcols, False, 0,
                                           dst_fn(128 * tt, 128))
                ps_free[pi], ps_eng[pi] = evv, wsem

        q_scr, k_scr = tens["q_scr"], tens["k_scr"]
        v_scr, lk_scr, lv_scr = tens["v_scr"], tens["lk_scr"], tens["lv_scr"]

        for pass_i, (h0, h1) in enumerate(passes):
            Th = h1 - h0
            ctx.we(nc.gpsimd, "mm", ctx.cnt["mm"])
            for q4 in range(4):
                dma = nc.gpsimd.dma_start(
                    out=xbf[:, 8 * q4:8 * (q4 + 1), 0:Th],
                    in_=tens["xT"].rearrange("(c p) t -> p c t", p=128)[
                        :, 8 * q4:8 * (q4 + 1), h0:h1])
                ctx.inc("xl", dma, 16)
            ctx.we(nc.gpsimd, "ev", ctx.cnt["ev"])
            dma = nc.gpsimd.dma_start(out=cos_sb[:, 0:Th],
                                      in_=tens["cosT"][:, h0:h1])
            ctx.inc("cs", dma, 16)
            dma = nc.gpsimd.dma_start(out=sin_sb[:, 0:Th],
                                      in_=tens["sinT"][:, h0:h1])
            ctx.inc("cs", dma, 16)

            proj_fm(tens["wlq_s"], 4, 32, xbf, False,
                    lambda fc, t0, tw: tens["lq_loc"].ap()[
                        fc * 128:(fc + 1) * 128, h0 + t0:h0 + t0 + tw],
                    Th)
            emit_pswitch_barrier(
                ctx, tens, barrier_id=pass_i, wait_sem="scr",
                copies_fn=lambda p: [
                    (tens["lq_sh"].ap()[512 * p:512 * (p + 1), h0:h1],
                     tens["lq_loc"].ap()[:, h0:h1])])
            ctx.we(nc.gpsimd, "mm", ctx.cnt["mm"])
            dma = nc.gpsimd.dma_start(
                out=lqbf[:, :, 0:Th],
                in_=tens["lq_sh"].ap().rearrange("(c p) t -> p c t", p=128)[
                    :, :, h0:h1])
            lqv = ctx.inc("lq", dma, 16)

            proj_fm(tens["wk_s"], 4, 32, xbf, True,
                    lambda fc, t0, tw: k_scr.ap()[
                        fc * 128:(fc + 1) * 128, h0 + t0:h0 + t0 + tw],
                    Th)
            proj_tm(tens["wv_s"].ap(), 512,
                    lambda t0, tr: v_scr.ap()[h0 + t0:h0 + t0 + tr, 0:512], Th)
            for fq in range(4):
                proj_tm(tens["wlv_s"].ap()[:, 512 * fq:512 * (fq + 1)], 512,
                        (lambda fq_:
                         lambda t0, tr: lv_scr.ap()[
                             h0 + t0:h0 + t0 + tr,
                             512 * fq_:512 * fq_ + 512])(fq),
                        Th)
            q0, q1 = max(h0, qlo), min(h1, qhi)
            if q1 > q0:
                off = q0 - h0
                assert off % 512 == 0
                xv = xbf[:, :, off:] if off else xbf
                proj_fm(tens["wq_s"], 16, 32, xv, True,
                        (lambda q0_, off_:
                         lambda fc, t0, tw: q_scr.ap()[
                             fc * 128:(fc + 1) * 128,
                             q0_ - qlo + t0:q0_ - qlo + t0 + tw])(q0, off),
                        q1 - q0, coff=off)
            ctx.we(nc.tensor, "lq", lqv)
            proj_fm(tens["wlk_s"], 16, 8, lqbf, True,
                    lambda fc, t0, tw: lk_scr.ap()[
                        fc * 128:(fc + 1) * 128, h0 + t0:h0 + t0 + tw],
                    Th)

        for eng in (nc.tensor, nc.sync, nc.gpsimd, nc.vector, nc.scalar):
            ctx.wait(eng, "scr")



def emit_phase_b(ctx, tens, role):
    nc = ctx.nc
    kend, qlo, qhi, _ = roles_params(role)
    Tq = qhi - qlo
    nqc = Tq // 512
    R = f"r{role}"

    with (
        nc.sbuf_tensor(f"kbuf{R}", [128, 2, S], BF16) as kbuf,
        nc.sbuf_tensor(f"vbuf{R}", [128, 2, 32, 128], BF16) as vbuf,
        nc.sbuf_tensor(f"klbuf{R}", [128, 2, S], BF16) as klbuf,
        nc.sbuf_tensor(f"vlbuf{R}", [128, 2, 32, 128], BF16) as vlbuf,
        nc.sbuf_tensor(f"qbuf{R}", [128, 2, 512], BF16) as qbuf,
        nc.sbuf_tensor(f"pbuf{R}", [128, 2, 1024], BF16) as pbuf,
        nc.sbuf_tensor(f"maskb{R}", [128, 2048], F32) as maskb,
        nc.sbuf_tensor(f"obuf{R}", [128, 2, 512], BF16) as obuf,
        nc.sbuf_tensor(f"ones_bf{R}", [128, 1], BF16) as ones_bf,
        nc.sbuf_tensor(f"ones1_f{R}", [1, 128], F32) as ones1_f,
        nc.sbuf_tensor(f"lnl{R}", [1, 512], F32) as lnl,
        nc.sbuf_tensor(f"recip{R}", [1, 512], F32) as recip,
        nc.sbuf_tensor(f"recipb{R}", [128, 512], F32) as recipb,
        nc.psum_tensor(f"psS{R}", [128, 2048], F32) as psS,
        nc.psum_tensor(f"psO{R}", [128, 1024], F32) as psO,
        nc.psum_tensor(f"psL{R}", [128, 1024], F32) as psL,
    ):
        v = nc.vector
        a = nc.scalar
        te = nc.tensor
        sy = nc.sync

        v.memset(ones_bf[:, :], 1.0)
        inst = v.memset(ones1_f[:, :], 1.0)
        ones_ev = ctx.inc("ev", inst, 1)
        dma = sy.dma_start(out=maskb[:, :], in_=tens["masks"][:, :])
        mask_kv = ctx.inc("kv", dma, 16)
        ctx.we(te, "ev", ones_ev)
        ctx.we(v, "kv", mask_kv)

        kv_free = [0, 0]
        lkv_free = [0, 0]
        q_free = [0, 0]
        pS_free = [0, 0]
        pb_free = [0, 0]
        oh_free = [0, 0]
        ohL_free = [0, 0]
        ob_free = [0, 0]
        kvload = [0]
        lkvload = [0]
        qload = [0]
        q_slot = [0]
        si = [0]

        def attend(j, qc, kind, kb_ap, vb_ap, kv_w):
            nkb = (qlo + 512 * (qc + 1)) // 128
            oh = si[0] % 2
            si[0] += 1
            li = j if kind == "std" else 16 + j
            first_pv = [True]
            mmv2 = 0
            for kb0 in range(0, nkb, 2):
                pair_n = min(2, nkb - kb0)
                ph = (kb0 // 2) % 2
                for i_ in range(pair_n):
                    kb = kb0 + i_
                    col0 = ph * 1024 + i_ * 512
                    ctx.we(te, "ac", pS_free[ph])
                    ctx.we(te, "kv", kv_w)
                    inst = te.matmul(
                        psS[:, col0:col0 + 512],
                        kb_ap[:, 128 * kb:128 * (kb + 1)],
                        qbuf[:, q_slot[0], 0:512],
                        start=True, stop=True)
                    mmv = ctx.inc("mm", inst, 1)
                evd = None
                for i_ in range(pair_n):
                    kb = kb0 + i_
                    mi = kb - (nkb - 4)
                    if mi >= 0:
                        col0 = ph * 1024 + i_ * 512
                        ctx.we(v, "mm", mmv)
                        inst = v.tensor_add(psS[:, col0:col0 + 512],
                                            psS[:, col0:col0 + 512],
                                            maskb[:, 512 * mi:512 * (mi + 1)])
                        evd = ctx.inc("ev", inst, 1)
                pb = (kb0 // 2) % 2
                ctx.we(a, "mm", mmv)
                if evd is not None:
                    ctx.we(a, "ev", evd)
                ctx.we(a, "mm", pb_free[pb])
                w = 512 * pair_n
                inst = a.activation(pbuf[:, pb, 0:w],
                                    psS[:, ph * 1024:ph * 1024 + w],
                                    AF.Exp, bias=f32(-SHIFT), scale=f32(SCALE))
                acv = ctx.inc("ac", inst, 1)
                pS_free[ph] = acv
                for i_ in range(pair_n):
                    kb = kb0 + i_
                    ctx.we(te, "ac", acv)
                    if first_pv[0]:
                        ctx.we(te, "ev", oh_free[oh])
                        ctx.we(te, "ac", ohL_free[oh])
                        first_pv[0] = False
                    inst = te.matmul(
                        psL[0:1, oh * 512:oh * 512 + 512],
                        ones_bf[:, :], pbuf[:, pb, 512 * i_:512 * i_ + 512],
                        start=(kb == 0), stop=(kb == nkb - 1))
                    ctx.inc("mm", inst, 1)
                    inst = te.matmul(
                        psO[:, oh * 512:oh * 512 + 512],
                        vb_ap[:, kb, :], pbuf[:, pb, 512 * i_:512 * i_ + 512],
                        start=(kb == 0), stop=(kb == nkb - 1))
                    mmv2 = ctx.inc("mm", inst, 1)
                pb_free[pb] = mmv2
            ctx.we(a, "mm", ctx.val("mm"))
            ctx.we(a, "ev", oh_free[1 - oh])
            inst = a.activation(lnl[0:1, 0:512],
                                psL[0:1, oh * 512:oh * 512 + 512], AF.Ln)
            acv = ctx.inc("ac", inst, 1)
            ctx.we(a, "ac", acv)
            inst = a.activation(recip[0:1, 0:512], lnl[0:1, 0:512],
                                AF.Exp, scale=f32(-1.0))
            acv = ctx.inc("ac", inst, 1)
            ctx.we(te, "ac", acv)
            inst = te.matmul(psL[:, oh * 512:oh * 512 + 512],
                             ones1_f[:, :], recip[0:1, 0:512],
                             start=True, stop=True)
            mmv = ctx.inc("mm", inst, 1)
            ctx.we(a, "mm", mmv)
            inst = a.copy(recipb[:, 0:512], psL[:, oh * 512:oh * 512 + 512])
            acv = ctx.inc("ac", inst, 1)
            ohL_free[oh] = acv
            ctx.we(v, "ac", acv)
            ctx.we(v, "ow", ob_free[oh])
            inst = v.tensor_mul(obuf[:, oh, 0:512],
                                psO[:, oh * 512:oh * 512 + 512],
                                recipb[:, 0:512])
            evv = ctx.inc("ev", inst, 1)
            oh_free[oh] = evv
            ctx.we(sy, "ev", evv)
            dma = sy.dma_start(
                out=tens["o_loc"].ap()[li * 128:(li + 1) * 128,
                                       512 * qc:512 * qc + 512],
                in_=obuf[:, oh, 0:512])
            ob_free[oh] = ctx.inc("ow", dma, 16)

        def load_q(j, qc):
            b = qload[0] % 2
            qload[0] += 1
            q_slot[0] = b
            ctx.we(sy, "mm", q_free[b])
            dma = sy.dma_start(
                out=qbuf[:, b, 0:512],
                in_=tens["q_scr"].ap()[j * 128:(j + 1) * 128,
                                       512 * qc:512 * qc + 512])
            return ctx.inc("kv", dma, 16)

        def load_kv(scr_k, scr_v, row0, nkb, kb_, vb_, free, loadcnt):
            b = loadcnt[0] % 2
            loadcnt[0] += 1
            ctx.we(sy, "mm", free[b])
            dma = sy.dma_start(
                out=kb_[:, b, 0:128 * nkb],
                in_=scr_k.ap()[row0:row0 + 128, 0:128 * nkb])
            ctx.inc("kv", dma, 16)
            dma = sy.dma_start(
                out=vb_[:, b, 0:nkb, :],
                in_=scr_v.ap().rearrange("(kb p) f -> p kb f", p=128)[
                    :, 0:nkb, row0:row0 + 128])
            kvv = ctx.inc("kv", dma, 16)
            return b, kvv

        for qc in range(nqc):
            nkb = (qlo + 512 * (qc + 1)) // 128
            for kvg in range(4):
                ctx.join()
                b, kvv = load_kv(tens["k_scr"], tens["v_scr"], 128 * kvg,
                                 nkb, kbuf, vbuf, kv_free, kvload)
                for j in range(4 * kvg, 4 * kvg + 4):
                    qv = load_q(j, qc)
                    attend(j, qc, "std", kbuf[:, b], vbuf[:, b],
                           max(kvv, qv))
                    kv_free[b] = ctx.val("mm")
                    q_free[q_slot[0]] = ctx.val("mm")
                    bl, lkvv = load_kv(tens["lk_scr"], tens["lv_scr"],
                                       128 * j, nkb, klbuf, vlbuf,
                                       lkv_free, lkvload)
                    attend(j, qc, "lat", klbuf[:, bl], vlbuf[:, bl],
                           max(lkvv, qv))
                    lkv_free[bl] = ctx.val("mm")
                    q_free[q_slot[0]] = ctx.val("mm")

        Tq_ = Tq
        emit_pswitch_barrier(
            ctx, tens, barrier_id=8, wait_sem="ow",
            copies_fn=lambda p: [
                (tens["o_sh"].ap()[2048 * p:2048 * (p + 1), 0:Tq_],
                 tens["o_loc"].ap()[0:2048, 0:Tq_]),
                (tens["o_sh"].ap()[4096 + 2048 * p:4096 + 2048 * (p + 1),
                                   0:Tq_],
                 tens["o_loc"].ap()[0 + 2048:4096, 0:Tq_]),
            ])


def emit_phase_cd(ctx, tens, role):
    nc = ctx.nc
    kend, qlo, qhi, _ = roles_params(role)
    myT = TCD[role]
    Tt = myT // 2
    R = f"r{role}"
    v, a, te, sy, g = nc.vector, nc.scalar, nc.tensor, nc.sync, nc.gpsimd
    tgs = [(0, 512), (512, Tt - 512)] if Tt > 512 else [(0, Tt)]

    psW_free = [(None, 0), (None, 0)]

    def wait_psW(slot):
        sem, val = psW_free[slot]
        if sem is not None:
            te.wait_ge(ctx.sem[sem], val)

    with (
        nc.sbuf_tensor(f"onesb2{R}", [128, 1], BF16) as onesb,
        nc.sbuf_tensor(f"ones12{R}", [1, 128], F32) as ones1,
        nc.sbuf_tensor(f"stat{R}", [1, 2560], F32) as stat,
        nc.sbuf_tensor(f"t16{R}", [128, 2, 512], BF16) as t16,
        nc.sbuf_tensor(f"t16q{R}", [128, 2, 512], BF16) as t16q,
        nc.sbuf_tensor(f"t32a{R}", [128, 640], F32) as t32a,
        nc.sbuf_tensor(f"ybf{R}", [128, 32, Tt], BF16) as ybf,
        nc.sbuf_tensor(f"ost{R}", [128, 2, 640], F32) as ost,
        nc.psum_tensor(f"psW0{R}", [128, 1024], F32) as psW0,
        nc.psum_tensor(f"psW1{R}", [128, 1024], F32) as psW1,
        nc.psum_tensor(f"psSt{R}", [1, 1024], F32) as psSt,
        nc.psum_tensor(f"psM{R}", [2, 1024], F32) as psM,
    ):
        psW = [psW0, psW1]
        inst = v.memset(onesb[:, :], 1.0)
        inst = v.memset(ones1[:, :], 1.0)
        ones_ev = ctx.inc("ev", inst, 1)
        ctx.we(te, "ev", ones_ev)

        MU, VAR, MSQ, STD, RSTD = 0, 512, 1024, 1536, 2048

        def emit_ln(src, dst_write, tc0):
            ctx.wait(a, "ev")
            for tgi, (o, w) in enumerate(tgs):
                ctx.wait(te, "ev")
                for dm in range(32):
                    sl = dm % 2
                    a.wait_ge(ctx.sem["mm"],
                              ctx.cnt.get(f"t16free{sl}", 0))
                    inst = a.copy(t16[:, sl, 0:w], src[:, dm, o:o + w])
                    ctx.inc("ac", inst, 1)
                    inst = a.activation(t16q[:, sl, 0:w], src[:, dm, o:o + w],
                                        AF.Square)
                    acv = ctx.inc("ac", inst, 1)
                    ctx.we(te, "ac", acv)
                    inst = te.matmul(psSt[0:1, 0:w], onesb[:, :],
                                     t16[:, sl, 0:w],
                                     start=(dm == 0), stop=(dm == 31))
                    ctx.inc("mm", inst, 1)
                    inst = te.matmul(psSt[0:1, 512:512 + w], onesb[:, :],
                                     t16q[:, sl, 0:w],
                                     start=(dm == 0), stop=(dm == 31))
                    mmv = ctx.inc("mm", inst, 1)
                    ctx.cnt[f"t16free{sl}"] = mmv
                ctx.we(v, "mm", mmv)
                v.tensor_scalar_mul(stat[0:1, MU:MU + w], psSt[0:1, 0:w],
                                    1.0 / 4096)
                v.tensor_scalar_mul(stat[0:1, VAR:VAR + w],
                                    psSt[0:1, 512:512 + w], 1.0 / 4096)
                v.tensor_mul(stat[0:1, MSQ:MSQ + w], stat[0:1, MU:MU + w],
                             stat[0:1, MU:MU + w])
                inst = v.tensor_sub(stat[0:1, VAR:VAR + w],
                                    stat[0:1, VAR:VAR + w],
                                    stat[0:1, MSQ:MSQ + w])
                evv = ctx.inc("ev", inst, 1)
                ctx.we(a, "ev", evv)
                inst = a.activation(stat[0:1, STD:STD + w],
                                    stat[0:1, VAR:VAR + w],
                                    AF.Sqrt, bias=f32(EPS))
                acv = ctx.inc("ac", inst, 1)
                ctx.we(v, "ac", acv)
                inst = v.reciprocal(stat[0:1, RSTD:RSTD + w],
                                    stat[0:1, STD:STD + w])
                evv = ctx.inc("ev", inst, 1)
                ctx.we(te, "ev", evv)
                wait_psW(0)
                wait_psW(1)
                inst = te.matmul(psW0[:, 0:w], ones1[:, :],
                                 stat[0:1, MU:MU + w], start=True, stop=True)
                ctx.inc("mm", inst, 1)
                inst = te.matmul(psW1[:, 0:w], ones1[:, :],
                                 stat[0:1, RSTD:RSTD + w],
                                 start=True, stop=True)
                mmv = ctx.inc("mm", inst, 1)
                for dm in range(32):
                    ctx.we(v, "mm", mmv)
                    v.tensor_sub(t32a[:, 0:w], src[:, dm, o:o + w],
                                 psW0[:, 0:w])
                    inst = v.tensor_mul(t32a[:, 0:w], t32a[:, 0:w],
                                        psW1[:, 0:w])
                    evv = ctx.inc("ev", inst, 1)
                    dst_write(dm, o, w, evv)
                psW_free[0] = ("ev", ctx.val("ev"))
                psW_free[1] = ("ev", ctx.val("ev"))

        def emit_c_half(th, tc0):
            with (
                nc.sbuf_tensor(f"oslice{R}{th}", [128, 16, Tt], BF16) as osl,
                nc.sbuf_tensor(f"cw{R}{th}", [128, 2, 16, 128], BF16) as cw,
                nc.sbuf_tensor(f"accC{R}{th}", [128, 32, Tt], F32) as acc,
                nc.sbuf_tensor(f"xbufc{R}{th}", [128, 2, 640], F32) as xbufc,
            ):
                cw_free = [0, 0]
                for ksb in range(4):
                    ctx.join()
                    ctx.we(g, "mm", ctx.cnt.get("oslfree", 0))
                    base = dict(ctx.cnt)
                    wcb = dict(ctx.wcache)
                    with g.If_eq(tens["pidx_pool_reg"], 0):
                        dma = g.dma_start(
                            out=osl[:, :, 0:Tt],
                            in_=tens["o_sh"].ap().rearrange(
                                "(kc p) t -> p kc t", p=128)[
                                :, 16 * ksb:16 * (ksb + 1), tc0:tc0 + Tt])
                        ctx.inc("ol", dma, 16)
                        saved = dict(ctx.cnt)
                    with g.Else():
                        ctx.cnt = base
                        ctx.wcache = dict(wcb)
                        dma = g.dma_start(
                            out=osl[:, :, 0:Tt],
                            in_=tens["o_sh"].ap().rearrange(
                                "(kc p) t -> p kc t", p=128)[
                                :, 16 * ksb:16 * (ksb + 1),
                                myT + tc0:myT + tc0 + Tt])
                        ctx.inc("ol", dma, 16)
                    assert saved == ctx.cnt
                    olv = ctx.val("ol")
                    for dm in range(32):
                        slot = dm % 2
                        ctx.we(g, "mm", cw_free[slot])
                        dma = g.dma_start(
                            out=cw[:, slot, :, :],
                            in_=tens["wcomb_dm"].ap()[
                                dm, 2048 * ksb:2048 * (ksb + 1), :].rearrange(
                                "(kc p) f -> p kc f", p=128))
                        wlv = ctx.inc("wl2", dma, 16)
                        wslot = dm % 2
                        wait_psW(wslot)
                        ctx.we(te, "wl2", wlv)
                        ctx.we(te, "ol", olv)
                        for tgi, (o, w) in enumerate(tgs):
                            for kc in range(16):
                                inst = te.matmul(
                                    psW[wslot][:, 512 * tgi:512 * tgi + w],
                                    cw[:, slot, kc, :],
                                    osl[:, kc, o:o + w],
                                    start=(kc == 0), stop=(kc == 15))
                        mmv = ctx.inc("mm", inst, 1)
                        cw_free[slot] = mmv
                        ctx.we(v, "mm", mmv)
                        if ksb == 0:
                            inst = v.tensor_copy(acc[:, dm, 0:Tt],
                                                 psW[wslot][:, 0:Tt])
                        else:
                            inst = v.tensor_add(acc[:, dm, 0:Tt],
                                                acc[:, dm, 0:Tt],
                                                psW[wslot][:, 0:Tt])
                        evv = ctx.inc("ev", inst, 1)
                        psW_free[wslot] = ("ev", evv)
                    ctx.cnt["oslfree"] = ctx.val("mm")
                for dm in range(32):
                    slot = dm % 2
                    ctx.we(sy, "ev", ctx.cnt.get(f"xbf_free{slot}", 0))
                    dma = sy.dma_start(
                        out=xbufc[:, slot, 0:Tt],
                        in_=tens["xTcd"][128 * dm:128 * (dm + 1),
                                         tc0:tc0 + Tt])
                    xlv = ctx.inc("xl", dma, 16)
                    ctx.we(v, "xl", xlv)
                    ctx.we(v, "ev", ctx.val("ev"))
                    inst = v.tensor_add(acc[:, dm, 0:Tt], acc[:, dm, 0:Tt],
                                        xbufc[:, slot, 0:Tt])
                    evv = ctx.inc("ev", inst, 1)
                    ctx.cnt[f"xbf_free{slot}"] = evv
                ystage_free = [(0, 0), (0, 0)]

                def ln1_write(dm, o, w, evv):
                    sl = dm % 2
                    scr_v, ac_v = ystage_free[sl]
                    ctx.we(v, "scr", scr_v)
                    ctx.we(v, "ac", ac_v)
                    inst = v.tensor_copy(ost[:, sl, 0:w], t32a[:, 0:w])
                    ev2 = ctx.inc("ev", inst, 1)
                    ctx.we(a, "ev", ev2)
                    inst = a.copy(ybf[:, dm, o:o + w], ost[:, sl, 0:w])
                    ac2 = ctx.inc("ac", inst, 1)
                    ctx.we(sy, "ev", ev2)
                    dma = sy.dma_start(
                        out=tens["y_scr"].ap()[128 * dm:128 * (dm + 1),
                                               tc0 + o:tc0 + o + w],
                        in_=ost[:, sl, 0:w])
                    ystage_free[sl] = (ctx.inc("scr", dma, 16), ac2)

                emit_ln(acc, ln1_write, tc0)

        def emit_d_half(th, tc0):
          with (
            nc.sbuf_tensor(f"zs{R}{th}", [128, 22, 640], BF16) as zs,
            nc.sbuf_tensor(f"midb{R}{th}", [2, 3, 640], BF16) as midb,
            nc.sbuf_tensor(f"b2b{R}{th}", [2, 4096], BF16) as b2b,
          ):
            with (
                nc.sbuf_tensor(f"xs{R}{th}", [128, 8, Tt], BF16) as xs,
                nc.sbuf_tensor(f"a13{R}{th}", [128, 32, 4], BF16) as a13,
                nc.sbuf_tensor(f"b1all{R}{th}", [2, 11264], BF16) as b1all,
                nc.sbuf_tensor(f"b3all{R}{th}", [2, 11264], BF16) as b3all,
                nc.sbuf_tensor(f"a2all{R}{th}", [128, 88, 2], BF16) as a2all,
                nc.sbuf_tensor(f"w1b{R}{th}", [128, 2, 8, 128], BF16) as w1b,
                nc.sbuf_tensor(f"w3b{R}{th}", [128, 2, 8, 128], BF16) as w3b,
                nc.sbuf_tensor(f"gbf{R}{th}", [128, 640], BF16) as gbf,
                nc.sbuf_tensor(f"ubf{R}{th}", [128, 640], BF16) as ubf,
                nc.sbuf_tensor(f"zbf{R}{th}", [128, 4, 640], BF16) as zbf,
            ):
                ctx.we(v, "ac", ctx.val("ac"))
                for i in range(8):
                    if i:
                        v.wait_ge(ctx.sem["ac"], ctx.val("ac"))
                    v.tensor_add(t32a[:, 0:Tt], ybf[:, i, 0:Tt],
                                 ybf[:, i + 8, 0:Tt])
                    v.tensor_add(t32a[:, 0:Tt], t32a[:, 0:Tt],
                                 ybf[:, i + 16, 0:Tt])
                    inst = v.tensor_add(t32a[:, 0:Tt], t32a[:, 0:Tt],
                                        ybf[:, i + 24, 0:Tt])
                    evv = ctx.inc("ev", inst, 1)
                    ctx.we(a, "ev", evv)
                    inst = a.copy(xs[:, i, 0:Tt], t32a[:, 0:Tt])
                    ctx.inc("ac", inst, 1)
                if DBG_FFN and th == 0:
                    ctx.we(g, "ac", ctx.val("ac"))
                    d1 = g.dma_start(out=tens["dbg_xs2"].ap()[:, :, 0:Tt],
                                     in_=xs[:, :, 0:Tt])
                    ctx.inc("fl", d1, 16)
                    d1 = g.dma_start(out=tens["dbg_ybf"].ap()[:, :, 0:Tt],
                                     in_=ybf[:, :, 0:Tt])
                    ctx.inc("fl", d1, 16)
                    ctx.we(g, "fl", ctx.val("fl"))
                dma = g.dma_start(out=a13[:, :, 0:2],
                                  in_=tens["a1T"].ap().rearrange(
                                      "(kc p) r -> p kc r", p=128))
                ctx.inc("wl2", dma, 16)
                dma = g.dma_start(out=a13[:, :, 2:4],
                                  in_=tens["a3T"].ap().rearrange(
                                      "(kc p) r -> p kc r", p=128))
                ctx.inc("wl2", dma, 16)
                dma = g.dma_start(
                    out=b1all[:, :],
                    in_=tens["b1_q"].ap().rearrange("r c m f -> r (c m f)"))
                ctx.inc("wl2", dma, 16)
                dma = g.dma_start(
                    out=b3all[:, :],
                    in_=tens["b3_q"].ap().rearrange("r c m f -> r (c m f)"))
                ctx.inc("wl2", dma, 16)
                dma = g.dma_start(
                    out=a2all[:, :, :],
                    in_=tens["a2_qm"].ap().rearrange("p c m r -> p (c m) r"))
                ctx.inc("wl2", dma, 16)
                dma = g.dma_start(out=b2b[:, :], in_=tens["b2T"][:, :])
                cstv = ctx.inc("wl2", dma, 16)
                ctx.join()
                for mi, acol in ((0, 0), (1, 2)):
                    for tgi, (o, w) in enumerate(tgs):
                        ctx.we(te, "wl2", cstv)
                        ctx.we(te, "ac", ctx.val("ac"))
                        for kc in range(32):
                            inst = te.matmul(
                                psM[0:2, 512 * tgi:512 * tgi + w],
                                a13[:, kc, acol:acol + 2],
                                ybf[:, kc, o:o + w],
                                start=(kc == 0), stop=(kc == 31))
                        mmv = ctx.inc("mm", inst, 1)
                        ctx.we(a, "mm", mmv)
                        inst = a.activation(midb[0:2, mi, o:o + w],
                                            psM[0:2, 512 * tgi:512 * tgi + w],
                                            AF.Copy, scale=f32(0.5))
                        ctx.inc("ac", inst, 1)
                ctx.join()
                w1_free = [0, 0]
                gup_free = [("ac", 0), ("ac", 0)]
                acv_cast = ctx.val("ac")
                for c in range(22):
                    ctx.join()
                    for m in range(4):
                        cm = 4 * c + m
                        slot = cm % 2
                        ctx.we(g, "mm", w1_free[slot])
                        dma = g.dma_start(
                            out=w1b[:, slot, :, :],
                            in_=tens["w1q_q"].ap()[c, m].rearrange(
                                "(kc p) f -> p kc f", p=128))
                        ctx.inc("wl2", dma, 16)
                        dma = g.dma_start(
                            out=w3b[:, slot, :, :],
                            in_=tens["w3q_q"].ap()[c, m].rearrange(
                                "(kc p) f -> p kc f", p=128))
                        wlv = ctx.inc("wl2", dma, 16)
                        ws = cm % 2
                        sem_, val_ = gup_free[ws]
                        te.wait_ge(ctx.sem[sem_], val_)
                        ctx.we(te, "wl2", wlv)
                        ctx.we(te, "ac", acv_cast)
                        for tgi, (o, w) in enumerate(tgs):
                            for kc in range(8):
                                te.matmul(
                                    psW[ws][:, 512 * tgi:512 * tgi + w],
                                    w1b[:, slot, kc, :], xs[:, kc, o:o + w],
                                    start=(kc == 0), stop=False)
                            inst = te.matmul(
                                psW[ws][:, 512 * tgi:512 * tgi + w],
                                b1all[0:2, 128 * cm:128 * (cm + 1)],
                                midb[0:2, 0, o:o + w],
                                start=False, stop=True)
                        mmv = ctx.inc("mm", inst, 1)
                        w1_free[slot] = mmv
                        ctx.we(a, "mm", mmv)
                        ctx.we(a, "ev", ctx.cnt.get("gu_free", 0))
                        inst = a.activation(gbf[:, 0:Tt], psW[ws][:, 0:Tt],
                                            AF.Silu)
                        g_ac = ctx.inc("ac", inst, 1)
                        gup_free[ws] = ("ac", g_ac)
                        ws2 = 1 - ws
                        sem_, val_ = gup_free[ws2]
                        te.wait_ge(ctx.sem[sem_], val_)
                        for tgi, (o, w) in enumerate(tgs):
                            for kc in range(8):
                                te.matmul(
                                    psW[ws2][:, 512 * tgi:512 * tgi + w],
                                    w3b[:, slot, kc, :], xs[:, kc, o:o + w],
                                    start=(kc == 0), stop=False)
                            inst = te.matmul(
                                psW[ws2][:, 512 * tgi:512 * tgi + w],
                                b3all[0:2, 128 * cm:128 * (cm + 1)],
                                midb[0:2, 1, o:o + w],
                                start=False, stop=True)
                        mmv = ctx.inc("mm", inst, 1)
                        w1_free[slot] = mmv
                        ctx.we(a, "mm", mmv)
                        ctx.we(a, "ac", g_ac)
                        ctx.we(a, "ev", ctx.cnt.get("gu_free", 0))
                        inst = a.copy(ubf[:, 0:Tt], psW[ws2][:, 0:Tt])
                        u_ac = ctx.inc("ac", inst, 1)
                        gup_free[ws2] = ("ac", u_ac)
                        ctx.we(v, "ac", u_ac)
                        v.wait_ge(ctx.sem["mm"],
                                  ctx.cnt.get(f"zfree{m}", 0))
                        inst = v.tensor_mul(zbf[:, m, 0:Tt], gbf[:, 0:Tt],
                                            ubf[:, 0:Tt])
                        z_ev = ctx.inc("ev", inst, 1)
                        ctx.cnt["gu_free"] = z_ev
                        ctx.we(te, "ev", z_ev)
                        for tgi, (o, w) in enumerate(tgs):
                            inst = te.matmul(
                                psM[0:2, 512 * tgi:512 * tgi + w],
                                a2all[:, cm, :], zbf[:, m, o:o + w],
                                start=(cm == 0), stop=(cm == 87))
                        mmv = ctx.inc("mm", inst, 1)
                        ctx.cnt[f"zfree{m}"] = mmv
                    ctx.we(v, "ev", ctx.val("ev"))
                    v.wait_ge(ctx.sem["ac"], ctx.val("ac"))
                    v.tensor_add(t32a[:, 0:Tt], zbf[:, 0, 0:Tt],
                                 zbf[:, 1, 0:Tt])
                    v.tensor_add(t32a[:, 0:Tt], t32a[:, 0:Tt],
                                 zbf[:, 2, 0:Tt])
                    inst = v.tensor_add(t32a[:, 0:Tt], t32a[:, 0:Tt],
                                        zbf[:, 3, 0:Tt])
                    evv = ctx.inc("ev", inst, 1)
                    ctx.we(a, "ev", evv)
                    inst = a.copy(zs[:, c, 0:Tt], t32a[:, 0:Tt])
                    zs_ac = ctx.inc("ac", inst, 1)
                ctx.we(a, "mm", ctx.val("mm"))
                inst = a.activation(midb[0:2, 2, 0:Tt], psM[0:2, 0:Tt],
                                    AF.Copy, scale=f32(0.5))
                mid2_ac = ctx.inc("ac", inst, 1)
                if DBG_FFN and th == 0:
                    ctx.we(g, "ac", ctx.val("ac"))
                    ctx.we(g, "ev", ctx.val("ev"))
                    d1 = g.dma_start(out=tens["dbg_zs"].ap()[:, :, 0:Tt],
                                     in_=zs[:, :, 0:Tt])
                    ctx.inc("fl", d1, 16)
                    d1 = g.dma_start(out=tens["dbg_xs"].ap()[:, :, 0:Tt],
                                     in_=xs[:, :, 0:Tt])
                    ctx.inc("fl", d1, 16)
                    d1 = g.dma_start(out=tens["dbg_midb"].ap()[:, :, 0:Tt],
                                     in_=midb[:, :, 0:Tt])
                    ctx.inc("fl", d1, 16)
                    ctx.we(g, "fl", ctx.val("fl"))
                last = (mid2_ac, zs_ac)
            ctx.join()
            emit_d2_half(th, tc0, zs, midb, b2b, last[0], last[1])

        def emit_d2_half(th, tc0, zs, midb, b2b, mid2_ac, zs_ac):
                with (
                    nc.sbuf_tensor(f"s2{R}{th}", [128, 32, Tt], F32) as s2,
                    nc.sbuf_tensor(f"w2c{R}{th}", [128, 2, 22, 128],
                                   BF16) as w2b,
                    nc.sbuf_tensor(f"ybk{R}{th}", [128, 2, 640], F32) as ybk,
                ):
                    ctx.join()
                    w2_free = [0, 0]
                    yb_free = [0, 0]
                    for dm in range(32):
                        slot = dm % 2
                        ctx.we(g, "mm", w2_free[slot])
                        dma = g.dma_start(
                            out=w2b[:, slot, :, :],
                            in_=tens["w2q_dm"].ap()[dm].rearrange(
                                "(kc p) f -> p kc f", p=128))
                        wlv = ctx.inc("wl2", dma, 16)
                        ctx.we(sy, "ev", yb_free[slot])
                        dma = sy.dma_start(
                            out=ybk[:, slot, 0:Tt],
                            in_=tens["y_scr"].ap()[128 * dm:128 * (dm + 1),
                                                   tc0:tc0 + Tt])
                        ylv = ctx.inc("xl", dma, 16)
                        ws = dm % 2
                        wait_psW(ws)
                        ctx.we(te, "wl2", wlv)
                        ctx.we(te, "ac", max(mid2_ac, zs_ac))
                        for tgi, (o, w) in enumerate(tgs):
                            for kc in range(22):
                                te.matmul(
                                    psW[ws][:, 512 * tgi:512 * tgi + w],
                                    w2b[:, slot, kc, :], zs[:, kc, o:o + w],
                                    start=(kc == 0), stop=False)
                            inst = te.matmul(
                                psW[ws][:, 512 * tgi:512 * tgi + w],
                                b2b[0:2, 128 * dm:128 * (dm + 1)],
                                midb[0:2, 2, o:o + w],
                                start=False, stop=True)
                        mmv = ctx.inc("mm", inst, 1)
                        w2_free[slot] = mmv
                        ctx.we(v, "mm", mmv)
                        ctx.we(v, "xl", ylv)
                        inst = v.tensor_add(s2[:, dm, 0:Tt],
                                            ybk[:, slot, 0:Tt],
                                            psW[ws][:, 0:Tt])
                        evv = ctx.inc("ev", inst, 1)
                        yb_free[slot] = evv
                        psW_free[ws] = ("ev", evv)
                    ctx.join()
                    ostage_free = [0, 0]

                    def ln2_write(dm, o, w, evv):
                        sl = dm % 2
                        ctx.we(v, "yw", ostage_free[sl])
                        inst = v.tensor_copy(ost[:, sl, 0:w], t32a[:, 0:w])
                        ev2 = ctx.inc("ev", inst, 1)
                        ctx.we(sy, "ev", ev2)
                        dma = sy.dma_start(
                            out=tens["y"].ap()[128 * dm:128 * (dm + 1),
                                               tc0 + o:tc0 + o + w],
                            in_=ost[:, sl, 0:w])
                        ostage_free[sl] = ctx.inc("yw", dma, 16)

                    emit_ln(s2, ln2_write, tc0)

        for th in range(2):
            emit_c_half(th, th * Tt)
            ctx.join()
            emit_d_half(th, th * Tt)
            ctx.join()

        ctx.we(sy, "yw", ctx.val("yw"))


def emit_debug_dump(ctx, tens, which):
    nc = ctx.nc
    for name in which:
        src = tens[name]
        dst = tens["dbg_" + name]
        dma = nc.gpsimd.dma_start(out=dst.ap(), in_=src.ap())
        ctx.inc("fl", dma, 16)
    ctx.we(nc.gpsimd, "fl", ctx.val("fl"))



def build_nc(phases=("A", "B", "CD"), debug_dump=()):
    nc = bass.Bass("TRN2", target_bir_lowering=False, debug=False, enable_asserts=False)

    tens = {}
    t = tens
    t["xT"] = nc.dram_tensor("xT", [D, S], F32, kind="ExternalInput").ap()
    t["wq_s"] = nc.dram_tensor("wq_s", [16, D, 128], F32, kind="ExternalInput").ap()
    t["wk_s"] = nc.dram_tensor("wk_s", [4, D, 128], F32, kind="ExternalInput").ap()
    t["wv_s"] = nc.dram_tensor("wv_s", [D, 512], F32, kind="ExternalInput")
    t["wlq_s"] = nc.dram_tensor("wlq_s", [4, D, 128], F32, kind="ExternalInput").ap()
    t["wlk_s"] = nc.dram_tensor("wlk_s", [16, LAT, 128], F32, kind="ExternalInput").ap()
    t["wlv_s"] = nc.dram_tensor("wlv_s", [D, 2048], F32, kind="ExternalInput")
    t["wcomb_dm"] = nc.dram_tensor("wcomb_dm", [32, 8192, 128], F32, kind="ExternalInput")
    t["w2q_dm"] = nc.dram_tensor("w2q_dm", [32, 2816, 128], F32, kind="ExternalInput")
    t["w1q_q"] = nc.dram_tensor("w1q_q", [22, 4, LAT, 128], F32, kind="ExternalInput")
    t["w3q_q"] = nc.dram_tensor("w3q_q", [22, 4, LAT, 128], F32, kind="ExternalInput")
    t["b1_q"] = nc.dram_tensor("b1_q", [2, 22, 4, 128], F32, kind="ExternalInput")
    t["b3_q"] = nc.dram_tensor("b3_q", [2, 22, 4, 128], F32, kind="ExternalInput")
    t["a2_qm"] = nc.dram_tensor("a2_qm", [128, 22, 4, 2], F32, kind="ExternalInput")
    t["xTcd"] = nc.dram_tensor("xTcd", [D, TCD[0]], F32, kind="ExternalInput")
    t["a1T"] = nc.dram_tensor("a1T", [D, 2], F32, kind="ExternalInput")
    t["a3T"] = nc.dram_tensor("a3T", [D, 2], F32, kind="ExternalInput")
    t["b2T"] = nc.dram_tensor("b2T", [2, D], F32, kind="ExternalInput")
    t["cosT"] = nc.dram_tensor("cosT", [128, S], F32, kind="ExternalInput").ap()
    t["sinT"] = nc.dram_tensor("sinT", [128, S], F32, kind="ExternalInput").ap()
    t["masks"] = nc.dram_tensor("masks", [128, 2048], F32, kind="ExternalInput")
    t["pslot"] = nc.dram_tensor("pslot", [1, 1], I32, kind="ExternalInput")
    t["y"] = nc.dram_tensor("y", [D, OUT_ROWS], F32, kind="ExternalOutput")

    t["q_scr"] = nc.dram_tensor("q_scr", [2048, S], BF16, kind="Internal")
    t["k_scr"] = nc.dram_tensor("k_scr", [512, S], BF16, kind="Internal")
    t["v_scr"] = nc.dram_tensor("v_scr", [S, 512], BF16, kind="Internal")
    t["lk_scr"] = nc.dram_tensor("lk_scr", [2048, S], BF16, kind="Internal")
    t["lv_scr"] = nc.dram_tensor("lv_scr", [S, 2048], BF16, kind="Internal")
    t["lq_loc"] = nc.dram_tensor("lq_loc", [512, S], BF16, kind="Internal")
    t["y_scr"] = nc.dram_tensor("y_scr", [D, TCD[0]], F32, kind="Internal")
    t["o_loc"] = nc.dram_tensor("o_loc", [4096, T_A], BF16, kind="Internal")
    t["lq_sh"] = nc.dram_tensor("lq_sh", [LAT, S], BF16, kind="Internal",
                                addr_space="Shared")
    t["o_sh"] = nc.dram_tensor("o_sh", [8192, T_A], BF16, kind="Internal",
                               addr_space="Shared")
    t["flags_sh"] = nc.dram_tensor("flags_sh", [16, 2], I32, kind="Internal",
                                   addr_space="Shared")
    if DBG_FFN:
        t["dbg_zs"] = nc.dram_tensor("dbg_zs", [128, 22, 640], F32,
                                     kind="ExternalOutput")
        t["dbg_xs"] = nc.dram_tensor("dbg_xs", [128, 8, 640], F32,
                                     kind="ExternalOutput")
        t["dbg_midb"] = nc.dram_tensor("dbg_midb", [2, 3, 640], F32,
                                       kind="ExternalOutput")
        t["dbg_xs2"] = nc.dram_tensor("dbg_xs2", [128, 8, 640], F32,
                                      kind="ExternalOutput")
        t["dbg_ybf"] = nc.dram_tensor("dbg_ybf", [128, 32, 640], F32,
                                      kind="ExternalOutput")
    for name in debug_dump:
        src = t[name]
        t["dbg_" + name] = nc.dram_tensor(
            "dbg_" + name, list(src.shape), F32, kind="ExternalOutput")

    ctx = Ctx(nc)
    for s in ("xl", "wl", "cs", "lq", "mm", "ev", "ac", "scr", "fl", "sp",
              "lqw", "kv", "ow", "ol", "wl2", "yw"):
        ctx.new_sem(s)

    pidb = nc.sbuf_tensor("pidb", [1, 1], I32).__enter__()
    t["magic_sb"] = nc.sbuf_tensor("magic_sb", [1, 1], I32).__enter__()
    t["flag_sb"] = nc.sbuf_tensor("flag_sb", [1, 1], I32).__enter__()

    engines = [nc.sync, nc.tensor, nc.vector, nc.scalar, nc.gpsimd]

    idx = []
    for eng in engines:
        pid = eng.compute_val(eng.partition_id())
        role = eng.compute_val(pid // 2 % 2)
        idx.append(role)

    preg = nc.gpsimd.alloc_register("pidx_pool")
    nc.gpsimd.reg_alu(preg, nc.gpsimd.partition_id(), 2, ALU.mod)
    t["pidx_pool_reg"] = preg

    spin_cnt = nc.gpsimd.alloc_register("spin_cnt")
    spin_nd = nc.gpsimd.alloc_register("spin_nd")
    nc.gpsimd.reg_mov(spin_cnt, 0)
    t["spin_cnt"] = spin_cnt
    t["spin_nd"] = spin_nd

    base_counts = dict(ctx.cnt)
    for role in nc.Switch(engines=engines, index=idx, n=2):
        ctx.cnt = dict(base_counts)
        ctx.wreset()
        if "A" in phases:
            emit_phase_a(ctx, tens, role)
            ctx.join()
        if "B" in phases:
            emit_phase_b(ctx, tens, role)
            ctx.join()
        if "CD" in phases:
            emit_phase_cd(ctx, tens, role)
        if debug_dump:
            emit_debug_dump(ctx, tens, debug_dump)

    return nc, ctx, tens


def kernel(**inputs):
    in_maps = host_prepare(inputs)
    nc, ctx, tens = build_nc()
    res = run_bass_kernel_spmd(nc, in_maps, core_ids=list(range(8)))
    out = np.empty((2, S, D), np.float32)
    for c in range(8):
        bb, role, p = c // 4, (c // 2) % 2, c % 2
        yT = res.results[c]["y"]
        if role == 0:
            t0, n = p * TCD[0], TCD[0]
        else:
            t0, n = S_A + p * TCD[1], TCD[1]
        out[bb, t0:t0 + n, :] = yT[:, 0:n].T
    return out
